# version 08
# baseline (speedup 1.0000x reference)
import contextlib

import numpy as np
import ml_dtypes

import concourse.bass as bass
import concourse.mybir as mybir
from concourse.bass_utils import run_bass_kernel_spmd

F32 = mybir.dt.float32
BF16 = mybir.dt.bfloat16
I8 = mybir.dt.int8
I32 = mybir.dt.int32
ALU = mybir.AluOpType
AF = mybir.ActivationFunctionType
BF16NP = np.dtype(ml_dtypes.bfloat16)

NCORE = 8
D = 4096
NDK = 32
NIB = 86
TH = 512
THIRDS = [(0, 29), (29, 58), (58, 86)]
NEPS = 1e-5


class Sem:
    def __init__(self, st, nc, name):
        self.h = st.enter_context(nc.semaphore(name))
        self.n = 0

    def inc(self, instr, amt=1):
        instr.then_inc(self.h, amt)
        self.n += amt
        return self.n


def build_nc():
    nc = bass.Bass("TRN2", target_bir_lowering=False)

    x_in = nc.dram_tensor("x", [1024, D], F32, kind="ExternalInput").ap()
    wg = nc.dram_tensor("wg", [NIB, 128, NDK, 128], I8, kind="ExternalInput").ap()
    wu = nc.dram_tensor("wu", [NIB, 128, NDK, 128], I8, kind="ExternalInput").ap()
    wd = nc.dram_tensor("wd", [32, 128, NIB, 128], I8, kind="ExternalInput").ap()
    wa = nc.dram_tensor("wa", [32, 128, 32, 128], F32, kind="ExternalInput").ap()
    mask = nc.dram_tensor("mask", [D, D], I32, kind="ExternalInput").ap()
    sgv_d = nc.dram_tensor("sgv", [128, NIB], F32, kind="ExternalInput").ap()
    suv_d = nc.dram_tensor("suv", [128, NIB], F32, kind="ExternalInput").ap()
    sdv_d = nc.dram_tensor("sdv", [128, 32], F32, kind="ExternalInput").ap()
    bia_d = nc.dram_tensor("bia", [128, 32], F32, kind="ExternalInput").ap()
    idf_d = nc.dram_tensor("idf", [128, 128], F32, kind="ExternalInput").ap()
    idb_d = nc.dram_tensor("idb", [128, 128], BF16, kind="ExternalInput").ap()
    one1_d = nc.dram_tensor("one1", [128, 1], BF16, kind="ExternalInput").ap()
    onek_d = nc.dram_tensor("onek", [1, 128], F32, kind="ExternalInput").ap()
    indr_d = nc.dram_tensor("indr", [128, 32 * 128], BF16, kind="ExternalInput").ap()
    wtmc = nc.dram_tensor("wtmc", [32, 2, 128, 2048], BF16).ap()
    out_d = nc.dram_tensor("out", [1024, D], F32, kind="ExternalOutput").ap()

    arena = nc.sbuf_tensor("ARENA", [128, 209216], mybir.dt.uint8)
    arena_h = arena.__enter__()
    base = nc.lookup_mloc(arena_h).addr

    def sb(name, shape, dtype, off):
        return nc.alloc_sbuf_tensor_at(name, shape, dtype, offset=base + off)

    R = sb("R", [128, NDK * TH], F32, 0)
    XN = sb("XN", [128, NDK * TH], BF16, 65536)
    H = sb("H", [128, 29 * TH], BF16, 98304)
    GS = sb("GS", [128, TH], F32, 128000)
    US = sb("US", [128, TH], F32, 130048)
    SG = sb("SG", [128, TH], F32, 132096)
    RB = sb("RB", [128, TH], F32, 134144)
    MB = sb("MB", [128, TH], F32, 136192)
    XB = sb("XB", [128, 2 * TH], BF16, 138240)
    XQ = sb("XQ", [128, 2 * TH], BF16, 140288)
    SGV = sb("SGV", [128, NIB], F32, 142336)
    SUV = sb("SUV", [128, NIB], F32, 142688)
    SDV = sb("SDV", [128, 32], F32, 143040)
    BIA = sb("BIA", [128, 32], F32, 143168)
    IDF = sb("IDF", [128, 128], F32, 143296)
    IDB = sb("IDB", [128, 128], BF16, 143808)
    ONE1 = sb("ONE1", [128, 1], BF16, 144064)
    ONEK = sb("ONEK", [1, 128], F32, 144096)
    BMEP = sb("BMEP", [128, 32 * 128], BF16, 144608)
    CS = sb("CS", [128, 128], F32, 152800)
    ST = sb("ST", [1, 4 * TH], F32, 153312)
    OUT0 = sb("OUT0", [128, 1024], F32, 153312)
    OUT1 = sb("OUT1", [128, 1024], F32, 157408)
    GBUF0 = sb("GBUF0", [128, NDK * 128], BF16, 161504)
    GBUF1 = sb("GBUF1", [128, NDK * 128], BF16, 169696)
    UBUF0 = sb("UBUF0", [128, NDK * 128], BF16, 177888)
    UBUF1 = sb("UBUF1", [128, NDK * 128], BF16, 186080)
    DBUF0 = sb("DBUF0", [128, 29 * 128], BF16, 194272)
    DBUF1 = sb("DBUF1", [128, 29 * 128], BF16, 201696)
    MB16A = sb("MB16A", [128, 4096], BF16, 161504)
    MB16B = sb("MB16B", [128, 4096], BF16, 169696)
    MB16C = sb("MB16C", [128, 4096], BF16, 177888)
    MB16D = sb("MB16D", [128, 4096], BF16, 186080)
    INDR = sb("INDR", [128, 32 * 128], BF16, 144608)
    BME2F = sb("BME2F", [128, 2048], F32, 186080)
    ABUF0 = sb("ABUF0", [128, 16 * 128], F32, 161504)
    ABUF1 = sb("ABUF1", [128, 16 * 128], F32, 169696)
    WTM0 = sb("WTM0", [128, 16 * 128], BF16, 177888)
    WTM1 = sb("WTM1", [128, 16 * 128], BF16, 181984)
    WTM2 = sb("WTM2", [128, 16 * 128], BF16, 186080)
    WTM3 = sb("WTM3", [128, 16 * 128], BF16, 190176)
    XIN0 = sb("XIN0", [128, 1024], F32, 186080)
    XIN1 = sb("XIN1", [128, 1024], F32, 190176)
    DBUF2 = sb("DBUF2", [128, 29 * 128], BF16, 153312)

    GB = [GBUF0, GBUF1]
    UB = [UBUF0, UBUF1]
    DB = [DBUF0, DBUF1, DBUF2]
    MB16 = [MB16A, MB16B, MB16C, MB16D]
    AB = [ABUF0, ABUF1]
    WTM = [WTM0, WTM1, WTM2, WTM3]
    XINB = [XIN0, XIN1]
    OUTB = [OUT0, OUT1]

    st = contextlib.ExitStack()
    with st:
        pall_cm = nc.psum_tensor("PALL", [128, 8 * 512], F32)
        PALL = pall_cm.__enter__()

        sem_t = Sem(st, nc, "sem_t")
        sem_v = Sem(st, nc, "sem_v")
        sem_s = Sem(st, nc, "sem_s")
        sem_d = [Sem(st, nc, "sem_d0"), Sem(st, nc, "sem_d1")]
        sem_a = [Sem(st, nc, "sem_a0"), Sem(st, nc, "sem_a1")]
        sem_g = [Sem(st, nc, "sem_g0"), Sem(st, nc, "sem_g1")]
        sem_n = [Sem(st, nc, "sem_n0"), Sem(st, nc, "sem_n1"),
                 Sem(st, nc, "sem_n2")]
        sem_c = [Sem(st, nc, "sem_c0"), Sem(st, nc, "sem_c1")]

        block = st.enter_context(nc.Block())
        mk = {}

        def init_dmas(sy):
            for dst, src in ((SGV, sgv_d), (SUV, suv_d), (SDV, sdv_d), (BIA, bia_d),
                             (IDF, idf_d), (IDB, idb_d), (ONE1, one1_d),
                             (ONEK, onek_d), (INDR, indr_d)):
                sem_d[0].inc(sy.dma_start(dst[:], src), 16)
        block.sync(init_dmas)
        d_init = sem_d[0].n

        def init_v(v):
            v.wait_ge(sem_d[0].h, d_init)
            sem_v.inc(v.tensor_scalar_mul(SDV[:], SDV[:], 0.5))
        block.vector(init_v)

        tmark_prep = {}
        psems = (sem_g[0], sem_g[1], sem_n[0], sem_n[1])
        for rt in range(32):
            par = rt % 4
            psem = psems[par]

            def pdma(gp, rt=rt, par=par, psem=psem):
                if rt < 4:
                    gp.wait_ge(sem_d[0].h, d_init)
                else:
                    gp.wait_ge(sem_t.h, tmark_prep[rt - 4])
                psem.inc(gp.dma_start(
                    MB16[par][:], mask[rt * 128:(rt + 1) * 128, :]), 16)
            block.gpsimd(pdma)
            dneed = psem.n

            def pmm(t, rt=rt, par=par, dneed=dneed, psem=psem):
                t.wait_ge(psem.h, dneed)
                ins = None
                for c in range(8):
                    ins = t.matmul(PALL[:, c * 512:(c + 1) * 512],
                                   INDR[:, rt * 128:(rt + 1) * 128],
                                   MB16[par][:, c * 512:(c + 1) * 512],
                                   start=(rt == 0), stop=(rt == 31))
                sem_t.inc(ins)
                tmark_prep[rt] = sem_t.n
            block.tensor(pmm)
        t_stageA = sem_t.n

        def stageB(v):
            v.wait_ge(sem_t.h, t_stageA)
            pb = PALL[:]
            va = bass.AP(pb.tensor, 0, [pb.ap[0], [32, 128]])
            sem_v.inc(v.tensor_copy(CS[:], va))
            for j in range(1, 32):
                v.wait_ge(sem_v.h, sem_v.n)
                vj = bass.AP(pb.tensor, j, [pb.ap[0], [32, 128]])
                sem_v.inc(v.tensor_add(CS[:], CS[:], vj))
            v.wait_ge(sem_v.h, sem_v.n)
            sem_v.inc(v.tensor_single_scalar(CS[:], CS[:], 0.0, ALU.is_gt))
        block.vector(stageB)
        v_thr = sem_v.n

        vmark_bmep = {}
        tmark_tp = {}
        for p2 in range(2):
            def expd(v, p2=p2):
                if p2 == 0:
                    v.wait_ge(sem_v.h, v_thr)
                else:
                    v.wait_ge(sem_t.h, tmark_tp[(p2 - 1, 15)])
                cs = CS[:]
                b2 = BME2F[:]
                csx = bass.AP(cs.tensor, cs.offset + p2 * 64,
                              [cs.ap[0], [1, 64], [0, 32]])
                b2x = bass.AP(b2.tensor, b2.offset, [b2.ap[0], [32, 64], [1, 32]])
                sem_v.inc(v.tensor_copy(b2x, csx))
            block.vector(expd)
            v_exp = sem_v.n

            for i in range(16):
                kt = p2 * 16 + i
                par = i % 2

                def tkt(t, p2=p2, i=i, kt=kt, par=par, v_exp=v_exp):
                    if i == 0:
                        t.wait_ge(sem_v.h, v_exp)
                    if i >= 2:
                        t.wait_ge(sem_v.h, vmark_bmep[kt - 2])
                    sem_t.inc(t.transpose(PALL[:, par * 512:par * 512 + 128],
                                          BME2F[:, i * 128:(i + 1) * 128], IDF[:]))
                    tmark_tp[(p2, i)] = sem_t.n
                block.tensor(tkt)
                tm = sem_t.n

                def ekt(v, kt=kt, par=par, tm=tm):
                    v.wait_ge(sem_t.h, tm)
                    sem_v.inc(v.tensor_copy(BMEP[:, kt * 128:(kt + 1) * 128],
                                            PALL[:, par * 512:par * 512 + 128]))
                block.vector(ekt)
                vmark_bmep[kt] = sem_v.n
        t_prep_done = sem_t.n

        pall_cm.__exit__(None, None, None)
        PB = [st.enter_context(nc.psum_tensor(f"PB{i}", [128, 512], F32))
              for i in range(8)]

        def emit_load(hf, t_xin_guard, stat_hook=None):
            v_entry, s_entry = sem_v.n, sem_s.n
            for qi in range(16):
                tt, q = qi // 4, qi % 4
                par = qi % 2

                c_ent_l = (sem_c[0].n, sem_c[1].n)

                def ldma(sy, hf=hf, tt=tt, q=q, qi=qi, par=par,
                         t_xin_guard=t_xin_guard, c_ent_l=c_ent_l):
                    if qi < 2:
                        sy.wait_ge(sem_t.h, t_xin_guard)
                        if c_ent_l[0]:
                            sy.wait_ge(sem_c[0].h, c_ent_l[0])
                        if c_ent_l[1]:
                            sy.wait_ge(sem_c[1].h, c_ent_l[1])
                    else:
                        sy.wait_ge(sem_t.h, mk[("t_xin", hf, qi - 2)])
                    row0 = hf * 512 + tt * 128
                    sem_d[par].inc(sy.dma_start(
                        XINB[par][:],
                        x_in[row0:row0 + 128, q * 1024:(q + 1) * 1024]), 16)
                block.sync(ldma)
                d_need = sem_d[par].n

                def ltp(t, hf=hf, qi=qi, par=par, d_need=d_need,
                        v_entry=v_entry, s_entry=s_entry):
                    t.wait_ge(sem_d[par].h, d_need)
                    if qi == 0:
                        t.wait_ge(sem_v.h, v_entry)
                        t.wait_ge(sem_s.h, s_entry)
                    else:
                        t.wait_ge(sem_v.h, mk[("v_xev", hf, qi - 1)])
                        t.wait_ge(sem_s.h, mk[("s_xev", hf, qi - 1)])
                    for j in range(8):
                        bank = PB[6] if j < 4 else PB[7]
                        ins = t.transpose(bank[:, (j % 4) * 128:(j % 4) * 128 + 128],
                                          XINB[par][:, j * 128:(j + 1) * 128], IDF[:])
                        if j == 3 or j == 7:
                            sem_t.inc(ins)
                    mk[("t_xin", hf, qi)] = sem_t.n
                block.tensor(ltp)
                t_need = sem_t.n

                def lev(v, hf=hf, tt=tt, q=q, qi=qi, t_need=t_need):
                    v.wait_ge(sem_t.h, t_need - 1)
                    ins = None
                    for j in range(4):
                        dk = q * 8 + j
                        ins = v.tensor_copy(
                            R[:, dk * TH + tt * 128: dk * TH + tt * 128 + 128],
                            PB[6][:, j * 128:(j + 1) * 128])
                    sem_v.inc(ins)
                    mk[("v_xev", hf, qi)] = sem_v.n
                block.vector(lev)

                def les(s, hf=hf, tt=tt, q=q, qi=qi, t_need=t_need):
                    s.wait_ge(sem_t.h, t_need)
                    ins = None
                    for j in range(4):
                        dk = q * 8 + 4 + j
                        ins = s.copy(
                            R[:, dk * TH + tt * 128: dk * TH + tt * 128 + 128],
                            PB[7][:, j * 128:(j + 1) * 128])
                    sem_s.inc(ins)
                    mk[("s_xev", hf, qi)] = sem_s.n
                block.scalar(les)
                if stat_hook is not None and qi >= 12:
                    q8 = (qi - 12) * 8
                    for dk in range(q8, q8 + 8):
                        stat_hook(dk, v_rgate=mk[("v_xev", hf, qi)],
                                  s_rgate=mk[("s_xev", hf, qi)])

        def ln_make(hf, sub):
            sa, bb_ = (4, 6) if sub == 1 else (6, 4)
            PSa, PSb = PB[sa], PB[sa + 1]
            PBa, PBb = PB[bb_], PB[bb_ + 1]
            s_entry = sem_s.n
            v_entry = sem_v.n
            t_entry = sem_t.n
            tmark_cast = []
            state = {"done": 0}

            def hook(dk, v_rgate=None, s_rgate=None):
                par = dk % 2

                def cst(s, dk=dk, par=par):
                    if v_rgate is not None:
                        s.wait_ge(sem_v.h, v_rgate)
                    if s_rgate is not None:
                        s.wait_ge(sem_s.h, s_rgate)
                    if dk == 0:
                        s.wait_ge(sem_s.h, s_entry)
                        if v_rgate is None:
                            s.wait_ge(sem_v.h, v_entry)
                    if dk < 2:
                        s.wait_ge(sem_t.h, t_entry)
                    else:
                        s.wait_ge(sem_t.h, tmark_cast[dk - 2])
                    sem_s.inc(s.activation(XB[:, par * TH:(par + 1) * TH],
                                           R[:, dk * TH:(dk + 1) * TH], AF.Copy))
                    sem_s.inc(s.activation(XQ[:, par * TH:(par + 1) * TH],
                                           R[:, dk * TH:(dk + 1) * TH], AF.Square))
                block.scalar(cst)
                s_need = sem_s.n

                def stm(t, dk=dk, par=par, s_need=s_need):
                    t.wait_ge(sem_s.h, s_need)
                    if dk == 0:
                        t.wait_ge(sem_v.h, v_entry)
                        t.wait_ge(sem_s.h, s_entry)
                    t.matmul(PSa[0:1, :], ONE1[:], XB[:, par * TH:(par + 1) * TH],
                             start=(dk == 0), stop=(dk == NDK - 1))
                    ins = t.matmul(PSb[0:1, :], ONE1[:],
                                   XQ[:, par * TH:(par + 1) * TH],
                                   start=(dk == 0), stop=(dk == NDK - 1))
                    sem_t.inc(ins)
                block.tensor(stm)
                tmark_cast.append(sem_t.n)
                state["done"] += 1

            def finalize(t_xn_guard, v_banks_guard=None, s_banks_guard=None):
                assert state["done"] == NDK
                t_stats = sem_t.n
                a_guard = (mk.get("a_out0", 0), mk.get("a_out1", 0))
                ST0 = ST[:, 0:512]
                ST1 = ST[:, 512:1024]
                ST2 = ST[:, 1024:1536]

                def stats_v1(v):
                    v.wait_ge(sem_t.h, t_stats)
                    v.wait_ge(sem_v.h, sem_v.n)
                    if a_guard[0]:
                        v.wait_ge(sem_a[0].h, a_guard[0])
                    if a_guard[1]:
                        v.wait_ge(sem_a[1].h, a_guard[1])
                    sem_v.inc(v.tensor_scalar_mul(ST0, PSa[0:1, :], 1.0 / D))
                    sem_v.inc(v.tensor_scalar_mul(ST1, PSb[0:1, :], 1.0 / D))
                    v.wait_ge(sem_v.h, sem_v.n)
                    sem_v.inc(v.tensor_mul(ST2, ST0, ST0))
                    v.wait_ge(sem_v.h, sem_v.n)
                    sem_v.inc(v.tensor_sub(ST1, ST1, ST2))
                    v.wait_ge(sem_v.h, sem_v.n)
                    sem_v.inc(v.tensor_scalar_add(ST1, ST1, NEPS))
                    v.wait_ge(sem_v.h, sem_v.n)
                    sem_v.inc(v.reciprocal(ST2, ST1))
                block.vector(stats_v1)
                v_need = sem_v.n

                def stats_s(s):
                    s.wait_ge(sem_v.h, v_need)
                    sem_s.inc(s.activation(ST2, ST2, AF.Sqrt))
                block.scalar(stats_s)
                s_need = sem_s.n

                def stats_v2(v):
                    v.wait_ge(sem_s.h, s_need)
                    if sub == 2:
                        sem_v.inc(v.tensor_mul(ST0, ST0, ST2))
                        v.wait_ge(sem_v.h, sem_v.n)
                    sem_v.inc(v.tensor_scalar_mul(ST0, ST0, -1.0))
                block.vector(stats_v2)
                v_stat2 = sem_v.n

                def bc_t(t):
                    t.wait_ge(sem_v.h, v_stat2)
                    if v_banks_guard:
                        t.wait_ge(sem_v.h, v_banks_guard)
                    if s_banks_guard:
                        t.wait_ge(sem_s.h, s_banks_guard)
                    t.matmul(PBa[:], ONEK[0:1, :], ST2, start=True, stop=True)
                    ins = t.matmul(PBb[:], ONEK[0:1, :], ST0, start=True, stop=True)
                    sem_t.inc(ins)
                block.tensor(bc_t)
                t_bc = sem_t.n

                def bc_v(v):
                    v.wait_ge(sem_t.h, t_bc)
                    sem_v.inc(v.tensor_copy(RB[:], PBa[:]))
                    sem_v.inc(v.tensor_copy(MB[:], PBb[:]))
                block.vector(bc_v)
                v_bc = sem_v.n
                mk[("v_bc", hf, sub)] = v_bc

                def apply_v(v):
                    v.wait_ge(sem_v.h, v_bc)
                    if t_xn_guard:
                        v.wait_ge(sem_t.h, t_xn_guard)
                    ins = None
                    for dk in range(NDK):
                        if sub == 2:
                            sem_v.inc(v.tensor_add(GS[:], R[:, dk * TH:(dk + 1) * TH],
                                                   MB[:]))
                            v.wait_ge(sem_v.h, sem_v.n)
                            sem_v.inc(v.tensor_mul(XN[:, dk * TH:(dk + 1) * TH],
                                                   GS[:], RB[:]))
                            v.wait_ge(sem_v.h, sem_v.n)
                        else:
                            ins = v.tensor_add(XN[:, dk * TH:(dk + 1) * TH],
                                               R[:, dk * TH:(dk + 1) * TH], MB[:])
                    if sub != 2:
                        sem_v.inc(ins)
                block.vector(apply_v)
                mk[("v_xn", hf, sub)] = sem_v.n

            return hook, finalize

        def emit_ffn(hf, pas, t_gu_region, v_gu_region, t_d_region,
                     t_up_region=None, stat_hook=None):
            v_xn = mk[("v_xn", hf, pas)]
            a_ent = (sem_a[0].n, sem_a[1].n)
            c_ent = (sem_c[0].n, sem_c[1].n)
            if t_up_region is None:
                t_up_region = t_gu_region
            for third, (ik0, ik1) in enumerate(THIRDS):
                nik = ik1 - ik0
                for ib in range(ik0, ik1):
                    par = ib % 2

                    def wdma(gp, hf=hf, pas=pas, ib=ib, par=par, third=third,
                             ik0=ik0):
                        if ib < 2:
                            gp.wait_ge(sem_t.h, t_gu_region)
                            if v_gu_region:
                                gp.wait_ge(sem_v.h, v_gu_region)
                            if a_ent[0]:
                                gp.wait_ge(sem_a[0].h, a_ent[0])
                            if a_ent[1]:
                                gp.wait_ge(sem_a[1].h, a_ent[1])
                            if c_ent[0]:
                                gp.wait_ge(sem_c[0].h, c_ent[0])
                            if c_ent[1]:
                                gp.wait_ge(sem_c[1].h, c_ent[1])
                        elif ib < ik0 + 2:
                            gp.wait_ge(sem_t.h, mk[("t_d", hf, pas, third * 32 - 12)])
                        else:
                            gp.wait_ge(sem_t.h, mk[("t_gu", hf, pas, ib - 2)])
                        sem_g[par].inc(gp.dma_start(GB[par][:], wg[ib]), 16)
                        if ib < 2:
                            gp.wait_ge(sem_t.h, t_up_region)
                        sem_g[par].inc(gp.dma_start(UB[par][:], wu[ib]), 16)
                    block.gpsimd(wdma)
                    gneed = sem_g[par].n

                    def gu_t(t, hf=hf, pas=pas, ib=ib, par=par, gneed=gneed,
                             v_xn=v_xn, ik0=ik0, third=third):
                        t.wait_ge(sem_g[par].h, gneed)
                        if ib < 2:
                            t.wait_ge(sem_v.h, v_xn)
                        else:
                            t.wait_ge(sem_v.h, mk[("v_guev", hf, pas, ib - 2)])
                        ins = None
                        for dk in range(NDK):
                            ins = t.matmul(PB[0 + 2 * par][:],
                                           GB[par][:, dk * 128:(dk + 1) * 128],
                                           XN[:, dk * TH:(dk + 1) * TH],
                                           start=(dk == 0), stop=(dk == NDK - 1))
                        sem_t.inc(ins)
                        for dk in range(NDK):
                            ins = t.matmul(PB[1 + 2 * par][:],
                                           UB[par][:, dk * 128:(dk + 1) * 128],
                                           XN[:, dk * TH:(dk + 1) * TH],
                                           start=(dk == 0), stop=(dk == NDK - 1))
                        sem_t.inc(ins)
                        mk[("t_gu", hf, pas, ib)] = sem_t.n
                    block.tensor(gu_t)
                    t_gate = sem_t.n - 1
                    t_up = sem_t.n

                    def gu_v1(v, hf=hf, pas=pas, ib=ib, par=par, t_gate=t_gate,
                              third=third, ik0=ik0):
                        v.wait_ge(sem_t.h, t_gate)
                        if third > 0 and ib == ik0:
                            v.wait_ge(sem_t.h, mk[("t_d", hf, pas, third * 32 - 1)])
                        v.wait_ge(sem_v.h, mk.get(("v_h", hf, pas, ib - 1), 0))
                        sem_v.inc(v.scalar_tensor_tensor(
                            GS[:], PB[0 + 2 * par][:], SGV[:, ib:ib + 1], RB[:],
                            ALU.mult, ALU.mult))
                    block.vector(gu_v1)
                    v_gs = sem_v.n

                    def gu_s(s, v_gs=v_gs):
                        s.wait_ge(sem_v.h, v_gs)
                        sem_s.inc(s.activation(SG[:], GS[:], AF.Silu))
                    block.scalar(gu_s)
                    s_sg = sem_s.n

                    def gu_v2(v, hf=hf, pas=pas, ib=ib, par=par, t_up=t_up,
                              s_sg=s_sg, ik0=ik0):
                        v.wait_ge(sem_t.h, t_up)
                        sem_v.inc(v.scalar_tensor_tensor(
                            US[:], PB[1 + 2 * par][:], SUV[:, ib:ib + 1], RB[:],
                            ALU.mult, ALU.mult))
                        mk[("v_guev", hf, pas, ib)] = sem_v.n
                        v.wait_ge(sem_v.h, sem_v.n)
                        v.wait_ge(sem_s.h, s_sg)
                        ikl = ib - ik0
                        sem_v.inc(v.tensor_mul(H[:, ikl * TH:(ikl + 1) * TH],
                                               SG[:], US[:]))
                        mk[("v_h", hf, pas, ib)] = sem_v.n
                    block.vector(gu_v2)
                if third == 2:
                    mk[("t_after_gu", hf, pas)] = sem_t.n
                v_h_third = sem_v.n

                for ot in range(32):
                    j = third * 32 + ot
                    spar = j % 3
                    bpar = j % 2

                    def ddma(gp, hf=hf, pas=pas, j=j, ot=ot, spar=spar, ik0=ik0,
                             ik1=ik1):
                        if j < 3:
                            gp.wait_ge(sem_t.h, t_d_region)
                            gp.wait_ge(sem_v.h, mk[("v_bc", hf, pas)])
                            if a_ent[0]:
                                gp.wait_ge(sem_a[0].h, a_ent[0])
                            if a_ent[1]:
                                gp.wait_ge(sem_a[1].h, a_ent[1])
                        else:
                            gp.wait_ge(sem_t.h, mk[("t_d", hf, pas, j - 3)])
                        sem_n[spar].inc(gp.dma_start(
                            DB[spar][:, 0:(ik1 - ik0) * 128],
                            wd[ot][:, ik0:ik1, :]), 16)
                    block.gpsimd(ddma)
                    gneed = sem_n[spar].n

                    def d_t(t, hf=hf, pas=pas, third=third, ot=ot, j=j, spar=spar,
                            bpar=bpar, gneed=gneed, nik=nik, v_h_third=v_h_third):
                        t.wait_ge(sem_n[spar].h, gneed)
                        if ot == 0:
                            t.wait_ge(sem_v.h, v_h_third)
                        if j < 2:
                            t.wait_ge(sem_v.h, mk[("v_bc", hf, pas)])
                        else:
                            t.wait_ge(sem_v.h, mk[("v_dev", hf, pas, j - 2)])
                        ins = None
                        for ik in range(nik):
                            ins = t.matmul(PB[4 + bpar][:],
                                           DB[spar][:, ik * 128:(ik + 1) * 128],
                                           H[:, ik * TH:(ik + 1) * TH],
                                           start=(ik == 0), stop=(ik == nik - 1))
                        sem_t.inc(ins)
                        mk[("t_d", hf, pas, j)] = sem_t.n
                    block.tensor(d_t)
                    t_dj = sem_t.n

                    def d_v(v, hf=hf, pas=pas, ot=ot, j=j, bpar=bpar, t_dj=t_dj):
                        v.wait_ge(sem_t.h, t_dj)
                        v.wait_ge(sem_v.h, sem_v.n)
                        sem_v.inc(v.scalar_tensor_tensor(
                            R[:, ot * TH:(ot + 1) * TH], PB[4 + bpar][:],
                            SDV[:, ot:ot + 1], R[:, ot * TH:(ot + 1) * TH],
                            ALU.mult, ALU.add))
                        mk[("v_dev", hf, pas, j)] = sem_v.n
                    block.vector(d_v)
                    if stat_hook is not None and third == 2 and ot >= 2:
                        stat_hook(ot - 2,
                                  v_rgate=mk[("v_dev", hf, pas, j - 2)])
            if stat_hook is not None:
                stat_hook(30, v_rgate=mk[("v_dev", hf, pas, 94)])
                stat_hook(31, v_rgate=mk[("v_dev", hf, pas, 95)])

        def emit_attn(hf, stat_hook=None):
            v_xn = mk[("v_xn", hf, 2)]
            t_gu1 = mk[("t_after_gu", hf, 1)]
            for nt in range(32):
                for kh in range(2):
                    sl = nt * 2 + kh
                    par = sl % 2
                    wp = sl % 4
                    qsem = (sem_d[0] if sl % 2 == 0 else sem_a[1]) if hf == 0 else \
                        (sem_d[0], sem_a[1], sem_d[1], sem_a[0])[sl % 4]
                    qblk = (block.sync if sl % 2 == 0 else block.scalar) if hf == 0 \
                        else (block.sync, block.scalar, block.sync,
                              block.scalar)[sl % 4]

                    if hf == 0:
                        def adma(eng, hf=hf, nt=nt, kh=kh, sl=sl, par=par,
                                 qsem=qsem, t_gu1=t_gu1):
                            if sl < 2:
                                eng.wait_ge(sem_t.h, t_gu1)
                            else:
                                eng.wait_ge(sem_v.h, mk[("v_msk", hf, sl - 2)])
                            qsem.inc(eng.dma_start(
                                AB[par][:], wa[nt][:, kh * 16:(kh + 1) * 16, :]), 16)
                        qblk(adma)
                        dneed = qsem.n

                        def msk(v, hf=hf, nt=nt, kh=kh, sl=sl, par=par,
                                dneed=dneed, qsem=qsem, t_gu1=t_gu1):
                            v.wait_ge(qsem.h, dneed)
                            if sl < 4:
                                v.wait_ge(sem_t.h, t_gu1)
                            else:
                                v.wait_ge(sem_t.h, mk[("t_am", hf, sl - 4)])
                                v.wait_ge(sem_c[sl % 2].h, mk[("a_wtm", hf, sl - 4)])
                            ins = None
                            for q in range(4):
                                kk0 = kh * 16 + q * 4
                                ab = AB[par][:]
                                wt = WTM[wp][:]
                                bp = BMEP[:]
                                in0 = bass.AP(ab.tensor, ab.offset + q * 512,
                                              [ab.ap[0], [128, 4], [32, 4], [1, 32]])
                                o0 = bass.AP(wt.tensor, wt.offset + q * 512,
                                             [wt.ap[0], [128, 4], [32, 4], [1, 32]])
                                i1 = bass.AP(bp.tensor,
                                             bp.offset + kk0 * 128 + nt * 4,
                                             [bp.ap[0], [128, 4], [1, 4], [0, 32]])
                                ins = v.tensor_tensor(o0, in0, i1, ALU.mult)
                            sem_v.inc(ins)
                            mk[("v_msk", hf, sl)] = sem_v.n
                        block.vector(msk)
                        v_m = sem_v.n

                        def cwr(s, hf=hf, nt=nt, kh=kh, sl=sl, wp=wp, v_m=v_m):
                            s.wait_ge(sem_v.h, v_m)
                            if sl >= 2:
                                s.wait_ge(sem_c[sl % 2].h, mk[("a_wtm", hf, sl - 2)])
                            sem_c[sl % 2].inc(
                                s.dma_start(wtmc[nt][kh], WTM[wp][:]), 16)
                            mk[("a_wtm", hf, sl)] = sem_c[sl % 2].n
                        block.scalar(cwr)
                        t_wtm_need = ("v", v_m)
                    else:
                        def adma(eng, hf=hf, nt=nt, kh=kh, sl=sl, wp=wp,
                                 qsem=qsem, t_gu1=t_gu1):
                            if sl < 4:
                                eng.wait_ge(sem_t.h, t_gu1)
                                eng.wait_ge(sem_c[sl % 2].h, mk[("a_wtm", 0, 62 + sl % 2)])
                            else:
                                eng.wait_ge(sem_t.h, mk[("t_am", hf, sl - 4)])
                            qsem.inc(eng.dma_start(
                                WTM[wp][:], wtmc[nt][kh]), 16)
                        qblk(adma)
                        t_wtm_need = ("q", qsem.n, qsem)

                    def amm(t, hf=hf, nt=nt, kh=kh, sl=sl, wp=wp,
                            t_wtm_need=t_wtm_need):
                        if t_wtm_need[0] == "v":
                            t.wait_ge(sem_v.h, t_wtm_need[1])
                        else:
                            t.wait_ge(t_wtm_need[2].h, t_wtm_need[1])
                        if nt < 2 and kh == 0:
                            t.wait_ge(sem_v.h, mk[("v_xn", hf, 2)])
                        elif kh == 0:
                            t.wait_ge(sem_v.h, mk[("v_aev", hf, nt - 2)])
                        ins = None
                        for k16 in range(16):
                            kk = kh * 16 + k16
                            ins = t.matmul(PB[4 + nt % 2][:],
                                           WTM[wp][:, k16 * 128:(k16 + 1) * 128],
                                           XN[:, kk * TH:(kk + 1) * TH],
                                           start=(kh == 0 and k16 == 0),
                                           stop=(kh == 1 and k16 == 15))
                        sem_t.inc(ins)
                        mk[("t_am", hf, sl)] = sem_t.n
                    block.tensor(amm)
                t_nt = sem_t.n

                def aev(v, hf=hf, nt=nt, t_nt=t_nt):
                    v.wait_ge(sem_t.h, t_nt)
                    v.wait_ge(sem_v.h, sem_v.n)
                    sem_v.inc(v.scalar_tensor_tensor(
                        R[:, nt * TH:(nt + 1) * TH], PB[4 + nt % 2][:],
                        BIA[:, nt:nt + 1], R[:, nt * TH:(nt + 1) * TH],
                        ALU.add, ALU.add))
                    mk[("v_aev", hf, nt)] = sem_v.n
                block.vector(aev)
                if stat_hook is not None and nt >= 2:
                    stat_hook(nt - 2, v_rgate=mk[("v_aev", hf, nt - 2)])
            if stat_hook is not None:
                stat_hook(30, v_rgate=mk[("v_aev", hf, 30)])
                stat_hook(31, v_rgate=mk[("v_aev", hf, 31)])
            mk[("t_after_attn", hf)] = sem_t.n

        def emit_store(hf):
            v_final = sem_v.n
            s_entry = sem_s.n
            a_entry = (mk.get("a_out0", 0), mk.get("a_out1", 0))
            for ci in range(16):
                tt, dq = ci // 4, ci % 4
                par = ci % 2

                def stp(t, hf=hf, tt=tt, dq=dq, ci=ci, v_final=v_final,
                        s_entry=s_entry):
                    if ci == 0:
                        t.wait_ge(sem_v.h, v_final)
                        t.wait_ge(sem_s.h, s_entry)
                    else:
                        t.wait_ge(sem_v.h, mk[("v_oev", hf, ci - 1)])
                        t.wait_ge(sem_s.h, mk[("s_oev", hf, ci - 1)])
                    for j in range(8):
                        dk = dq * 8 + j
                        bank = PB[6] if j < 4 else PB[7]
                        ins = t.transpose(
                            bank[:, (j % 4) * 128:(j % 4) * 128 + 128],
                            R[:, dk * TH + tt * 128: dk * TH + tt * 128 + 128],
                            IDF[:])
                        if j == 3 or j == 7:
                            sem_t.inc(ins)
                    mk[("t_otp", hf, ci)] = sem_t.n
                block.tensor(stp)
                t_need = sem_t.n

                def sev(v, hf=hf, ci=ci, par=par, t_need=t_need, a_entry=a_entry):
                    v.wait_ge(sem_t.h, t_need - 1)
                    v.wait_ge(sem_a[par].h, mk[("a_ost", hf, ci - 2)] if ci >= 2
                              else a_entry[par])
                    sem_v.inc(v.tensor_copy(OUTB[par][:, 0:512], PB[6][:, 0:512]))
                    mk[("v_oev", hf, ci)] = sem_v.n
                block.vector(sev)
                v_need = sem_v.n

                def ses(s, hf=hf, tt=tt, dq=dq, ci=ci, par=par, t_need=t_need,
                        v_need=v_need, a_entry=a_entry):
                    s.wait_ge(sem_t.h, t_need)
                    s.wait_ge(sem_a[par].h, mk[("a_ost", hf, ci - 2)] if ci >= 2
                              else a_entry[par])
                    sem_s.inc(s.copy(OUTB[par][:, 512:1024], PB[7][:, 0:512]))
                    mk[("s_oev", hf, ci)] = sem_s.n
                    s.wait_ge(sem_s.h, mk[("s_oev", hf, ci)])
                    s.wait_ge(sem_v.h, v_need)
                    row0 = hf * 512 + tt * 128
                    sem_a[par].inc(s.dma_start(
                        out_d[row0:row0 + 128, dq * 1024:(dq + 1) * 1024],
                        OUTB[par][:]), 16)
                    mk[("a_ost", hf, ci)] = sem_a[par].n
                block.scalar(ses)
            mk["a_out0"] = sem_a[0].n
            mk["a_out1"] = sem_a[1].n

        for hf in range(2):
            t_xin_guard = t_prep_done if hf == 0 else mk[("t_after_gu", 0, 3)]
            ln1_hook, ln1_fin = ln_make(hf, 1)
            emit_load(hf, t_xin_guard, stat_hook=ln1_hook)
            t_load = sem_t.n
            ln1_fin(t_xn_guard=(0 if hf == 0 else mk[("t_after_gu", 0, 3)]),
                    v_banks_guard=mk[("v_xev", hf, 15)],
                    s_banks_guard=mk[("s_xev", hf, 15)])
            ln2_hook, ln2_fin = ln_make(hf, 2)
            emit_ffn(hf, 1,
                     t_gu_region=t_load,
                     v_gu_region=(0 if hf == 0 else mk[("v_msk", 0, 63)]),
                     t_d_region=t_load,
                     stat_hook=ln2_hook)
            ln2_fin(t_xn_guard=mk[("t_after_gu", hf, 1)],
                    v_banks_guard=mk[("v_dev", hf, 1, 95)])
            ln3_hook, ln3_fin = ln_make(hf, 3)
            emit_attn(hf, stat_hook=ln3_hook)
            ln3_fin(t_xn_guard=mk[("t_after_attn", hf)],
                    v_banks_guard=mk[("v_aev", hf, 31)])
            emit_ffn(hf, 3,
                     t_gu_region=(mk[("v_msk", hf, 63)] * 0 + mk[("t_after_attn", hf)]
                                  if hf == 0 else mk[("t_after_gu", hf, 1)]),
                     v_gu_region=(mk[("v_msk", hf, 63)] if hf == 0 else 0),
                     t_d_region=mk[("t_d", hf, 1, 95)],
                     t_up_region=mk[("t_after_attn", hf)])
            emit_store(hf)

        a_all = (sem_a[0].n, sem_a[1].n)

        def fin(s, a_all=a_all):
            s.wait_ge(sem_a[0].h, a_all[0])
            s.wait_ge(sem_a[1].h, a_all[1])
        block.scalar(fin)

    return nc


def _layout_inputs(L0_attn_data, L0_attn_mask, L0_attn_weight, L0_attn_bias,
                   L0_ffn_w_gate, L0_ffn_s_gate, L0_ffn_w_up, L0_ffn_s_up,
                   L0_ffn_w_down, L0_ffn_s_down):
    x = np.ascontiguousarray(np.asarray(L0_attn_data, dtype=np.float32)
                             .reshape(8192, D))
    wg = np.ascontiguousarray(np.asarray(L0_ffn_w_gate, dtype=np.int8)
                              .reshape(NIB, 128, NDK, 128).transpose(0, 3, 2, 1))
    wu = np.ascontiguousarray(np.asarray(L0_ffn_w_up, dtype=np.int8)
                              .reshape(NIB, 128, NDK, 128).transpose(0, 3, 2, 1))
    wd = np.ascontiguousarray(np.asarray(L0_ffn_w_down, dtype=np.int8)
                              .reshape(32, 128, NIB, 128).transpose(0, 3, 2, 1))
    wa = np.ascontiguousarray(np.asarray(L0_attn_weight, dtype=np.float32)
                              .reshape(32, 128, 32, 128).transpose(0, 3, 2, 1))
    mask = np.ascontiguousarray(np.asarray(L0_attn_mask, dtype=np.int32))
    sgv = np.ascontiguousarray(np.asarray(L0_ffn_s_gate, dtype=np.float32)
                               .reshape(NIB, 128).T)
    suv = np.ascontiguousarray(np.asarray(L0_ffn_s_up, dtype=np.float32)
                               .reshape(NIB, 128).T)
    sdv = np.ascontiguousarray(np.asarray(L0_ffn_s_down, dtype=np.float32)
                               .reshape(32, 128).T)
    bia = np.ascontiguousarray(np.asarray(L0_attn_bias, dtype=np.float32)
                               .reshape(32, 128).T)
    idf = np.eye(128, dtype=np.float32)
    idb = np.eye(128, dtype=np.float32).astype(BF16NP)
    one1 = np.ones((128, 1), dtype=np.float32).astype(BF16NP)
    onek = np.ones((1, 128), dtype=np.float32)
    indr = np.zeros((128, 32 * 128), dtype=np.float32)
    p = np.arange(128)
    for rt in range(32):
        indr[p, rt * 128 + rt * 4 + p // 32] = 1.0
    indr = indr.astype(BF16NP)
    shared = dict(wg=wg, wu=wu, wd=wd, wa=wa, mask=mask, sgv=sgv, suv=suv,
                  sdv=sdv, bia=bia, idf=idf, idb=idb, one1=one1, onek=onek,
                  indr=indr)
    in_maps = []
    for c in range(NCORE):
        m = dict(shared)
        m["x"] = np.ascontiguousarray(x[c * 1024:(c + 1) * 1024])
        in_maps.append(m)
    return in_maps


def kernel(L0_attn_data, L0_attn_mask, L0_attn_weight, L0_attn_bias,
           L0_ffn_w_gate, L0_ffn_s_gate, L0_ffn_w_up, L0_ffn_s_up,
           L0_ffn_w_down, L0_ffn_s_down):
    in_maps = _layout_inputs(L0_attn_data, L0_attn_mask, L0_attn_weight,
                             L0_attn_bias, L0_ffn_w_gate, L0_ffn_s_gate,
                             L0_ffn_w_up, L0_ffn_s_up, L0_ffn_w_down,
                             L0_ffn_s_down)
    nc = build_nc()
    res = run_bass_kernel_spmd(nc, in_maps, core_ids=list(range(NCORE)))
    out = np.concatenate([np.asarray(res.results[c]["out"]) for c in range(NCORE)],
                         axis=0)
    return np.ascontiguousarray(out.reshape(2, 4096, D), dtype=np.float32)


# version 09
# speedup vs baseline: 1.0838x; 1.0838x over previous
import contextlib

import numpy as np
import ml_dtypes

import concourse.bass as bass
import concourse.mybir as mybir
from concourse.bass_utils import run_bass_kernel_spmd

F32 = mybir.dt.float32
BF16 = mybir.dt.bfloat16
I8 = mybir.dt.int8
I32 = mybir.dt.int32
ALU = mybir.AluOpType
AF = mybir.ActivationFunctionType
BF16NP = np.dtype(ml_dtypes.bfloat16)

NCORE = 8
D = 4096
NDK = 32
NIB = 86
TH = 512
THIRDS = [(0, 29), (29, 58), (58, 86)]
NEPS = 1e-5


class Sem:
    def __init__(self, st, nc, name):
        self.h = st.enter_context(nc.semaphore(name))
        self.n = 0

    def inc(self, instr, amt=1):
        instr.then_inc(self.h, amt)
        self.n += amt
        return self.n


def build_nc():
    nc = bass.Bass("TRN2", target_bir_lowering=False)

    x_in = nc.dram_tensor("x", [1024, D], F32, kind="ExternalInput").ap()
    wg = nc.dram_tensor("wg", [NIB, 128, NDK, 128], I8, kind="ExternalInput").ap()
    wu = nc.dram_tensor("wu", [NIB, 128, NDK, 128], I8, kind="ExternalInput").ap()
    wd = nc.dram_tensor("wd", [32, 128, NIB, 128], I8, kind="ExternalInput").ap()
    wa = nc.dram_tensor("wa", [32, 128, 32, 128], F32, kind="ExternalInput").ap()
    mask = nc.dram_tensor("mask", [D, D], I32, kind="ExternalInput").ap()
    sgv_d = nc.dram_tensor("sgv", [128, NIB], F32, kind="ExternalInput").ap()
    suv_d = nc.dram_tensor("suv", [128, NIB], F32, kind="ExternalInput").ap()
    sdv_d = nc.dram_tensor("sdv", [128, 32], F32, kind="ExternalInput").ap()
    bia_d = nc.dram_tensor("bia", [128, 32], F32, kind="ExternalInput").ap()
    idf_d = nc.dram_tensor("idf", [128, 128], F32, kind="ExternalInput").ap()
    idb_d = nc.dram_tensor("idb", [128, 128], BF16, kind="ExternalInput").ap()
    one1_d = nc.dram_tensor("one1", [128, 1], BF16, kind="ExternalInput").ap()
    onek_d = nc.dram_tensor("onek", [1, 128], F32, kind="ExternalInput").ap()
    indr_d = nc.dram_tensor("indr", [128, 32 * 128], BF16, kind="ExternalInput").ap()
    wtmc = nc.dram_tensor("wtmc", [32, 2, 128, 2048], BF16).ap()
    out_d = nc.dram_tensor("out", [1024, D], F32, kind="ExternalOutput").ap()

    arena = nc.sbuf_tensor("ARENA", [128, 209216], mybir.dt.uint8)
    arena_h = arena.__enter__()
    base = nc.lookup_mloc(arena_h).addr

    def sb(name, shape, dtype, off):
        return nc.alloc_sbuf_tensor_at(name, shape, dtype, offset=base + off)

    R = sb("R", [128, NDK * TH], F32, 0)
    XN = sb("XN", [128, NDK * TH], BF16, 65536)
    H = sb("H", [128, 29 * TH], BF16, 98304)
    GS = sb("GS", [128, TH], F32, 128000)
    US = sb("US", [128, TH], F32, 130048)
    SG = sb("SG", [128, TH], F32, 132096)
    RB = sb("RB", [128, TH], F32, 134144)
    MB = sb("MB", [128, TH], F32, 136192)
    XB = sb("XB", [128, 2 * TH], BF16, 138240)
    XQ = sb("XQ", [128, 2 * TH], BF16, 140288)
    SGV = sb("SGV", [128, NIB], F32, 142336)
    SUV = sb("SUV", [128, NIB], F32, 142688)
    SDV = sb("SDV", [128, 32], F32, 143040)
    BIA = sb("BIA", [128, 32], F32, 143168)
    IDF = sb("IDF", [128, 128], F32, 143296)
    IDB = sb("IDB", [128, 128], BF16, 143808)
    ONE1 = sb("ONE1", [128, 1], BF16, 144064)
    ONEK = sb("ONEK", [1, 128], F32, 144096)
    BMEP = sb("BMEP", [128, 32 * 128], BF16, 144608)
    CS = sb("CS", [128, 128], F32, 152800)
    ST = sb("ST", [1, 4 * TH], F32, 153312)
    OUT0 = sb("OUT0", [128, 1024], F32, 153312)
    OUT1 = sb("OUT1", [128, 1024], F32, 157408)
    GBUF0 = sb("GBUF0", [128, NDK * 128], BF16, 161504)
    GBUF1 = sb("GBUF1", [128, NDK * 128], BF16, 169696)
    UBUF0 = sb("UBUF0", [128, NDK * 128], BF16, 177888)
    UBUF1 = sb("UBUF1", [128, NDK * 128], BF16, 186080)
    DBUF0 = sb("DBUF0", [128, 29 * 128], BF16, 194272)
    DBUF1 = sb("DBUF1", [128, 29 * 128], BF16, 201696)
    MB16A = sb("MB16A", [128, 4096], BF16, 161504)
    MB16B = sb("MB16B", [128, 4096], BF16, 169696)
    MB16C = sb("MB16C", [128, 4096], BF16, 177888)
    MB16D = sb("MB16D", [128, 4096], BF16, 186080)
    INDR = sb("INDR", [128, 32 * 128], BF16, 144608)
    BME2F = sb("BME2F", [128, 2048], F32, 186080)
    ABUF0 = sb("ABUF0", [128, 16 * 128], F32, 161504)
    ABUF1 = sb("ABUF1", [128, 16 * 128], F32, 169696)
    WTM0 = sb("WTM0", [128, 16 * 128], BF16, 177888)
    WTM1 = sb("WTM1", [128, 16 * 128], BF16, 181984)
    WTM2 = sb("WTM2", [128, 16 * 128], BF16, 186080)
    WTM3 = sb("WTM3", [128, 16 * 128], BF16, 190176)
    XIN0 = sb("XIN0", [128, 1024], F32, 186080)
    XIN1 = sb("XIN1", [128, 1024], F32, 190176)
    DBUF2 = sb("DBUF2", [128, 29 * 128], BF16, 153312)

    GB = [GBUF0, GBUF1]
    UB = [UBUF0, UBUF1]
    DB = [DBUF0, DBUF1, DBUF2]
    MB16 = [MB16A, MB16B, MB16C, MB16D]
    AB = [ABUF0, ABUF1]
    WTM = [WTM0, WTM1, WTM2, WTM3]
    XINB = [XIN0, XIN1]
    OUTB = [OUT0, OUT1]

    st = contextlib.ExitStack()
    with st:
        pall_cm = nc.psum_tensor("PALL", [128, 8 * 512], F32)
        PALL = pall_cm.__enter__()

        sem_t = Sem(st, nc, "sem_t")
        sem_v = Sem(st, nc, "sem_v")
        sem_s = Sem(st, nc, "sem_s")
        sem_d = [Sem(st, nc, "sem_d0"), Sem(st, nc, "sem_d1")]
        sem_a = [Sem(st, nc, "sem_a0"), Sem(st, nc, "sem_a1")]
        sem_g = [Sem(st, nc, "sem_g0"), Sem(st, nc, "sem_g1")]
        sem_n = [Sem(st, nc, "sem_n0"), Sem(st, nc, "sem_n1"),
                 Sem(st, nc, "sem_n2")]
        sem_c = [Sem(st, nc, "sem_c0"), Sem(st, nc, "sem_c1")]

        block = st.enter_context(nc.Block())
        mk = {}

        def init_dmas(sy):
            for dst, src in ((SGV, sgv_d), (SUV, suv_d), (SDV, sdv_d), (BIA, bia_d),
                             (IDF, idf_d), (IDB, idb_d), (ONE1, one1_d),
                             (ONEK, onek_d), (INDR, indr_d)):
                sem_d[0].inc(sy.dma_start(dst[:], src), 16)
        block.sync(init_dmas)
        d_init = sem_d[0].n

        def init_v(v):
            v.wait_ge(sem_d[0].h, d_init)
            sem_v.inc(v.tensor_scalar_mul(SDV[:], SDV[:], 0.5))
        block.vector(init_v)

        tmark_prep = {}
        psems = (sem_g[0], sem_g[1], sem_n[0], sem_n[1])
        for rt in range(32):
            par = rt % 4
            psem = psems[par]

            def pdma(gp, rt=rt, par=par, psem=psem):
                if rt < 4:
                    gp.wait_ge(sem_d[0].h, d_init)
                else:
                    gp.wait_ge(sem_t.h, tmark_prep[rt - 4])
                psem.inc(gp.dma_start(
                    MB16[par][:], mask[rt * 128:(rt + 1) * 128, :]), 16)
            block.gpsimd(pdma)
            dneed = psem.n

            def pmm(t, rt=rt, par=par, dneed=dneed, psem=psem):
                t.wait_ge(psem.h, dneed)
                ins = None
                for c in range(8):
                    ins = t.matmul(PALL[:, c * 512:(c + 1) * 512],
                                   INDR[:, rt * 128:(rt + 1) * 128],
                                   MB16[par][:, c * 512:(c + 1) * 512],
                                   start=(rt == 0), stop=(rt == 31))
                sem_t.inc(ins)
                tmark_prep[rt] = sem_t.n
            block.tensor(pmm)
        t_stageA = sem_t.n

        def stageB(v):
            v.wait_ge(sem_t.h, t_stageA)
            pb = PALL[:]
            va = bass.AP(pb.tensor, 0, [pb.ap[0], [32, 128]])
            sem_v.inc(v.tensor_copy(CS[:], va))
            for j in range(1, 32):
                v.wait_ge(sem_v.h, sem_v.n)
                vj = bass.AP(pb.tensor, j, [pb.ap[0], [32, 128]])
                sem_v.inc(v.tensor_add(CS[:], CS[:], vj))
            v.wait_ge(sem_v.h, sem_v.n)
            sem_v.inc(v.tensor_single_scalar(CS[:], CS[:], 0.0, ALU.is_gt))
        block.vector(stageB)
        v_thr = sem_v.n

        vmark_bmep = {}
        tmark_tp = {}
        for p2 in range(2):
            def expd(v, p2=p2):
                if p2 == 0:
                    v.wait_ge(sem_v.h, v_thr)
                else:
                    v.wait_ge(sem_t.h, tmark_tp[(p2 - 1, 15)])
                cs = CS[:]
                b2 = BME2F[:]
                csx = bass.AP(cs.tensor, cs.offset + p2 * 64,
                              [cs.ap[0], [1, 64], [0, 32]])
                b2x = bass.AP(b2.tensor, b2.offset, [b2.ap[0], [32, 64], [1, 32]])
                sem_v.inc(v.tensor_copy(b2x, csx))
            block.vector(expd)
            v_exp = sem_v.n

            for i in range(16):
                kt = p2 * 16 + i
                par = i % 2

                def tkt(t, p2=p2, i=i, kt=kt, par=par, v_exp=v_exp):
                    if i == 0:
                        t.wait_ge(sem_v.h, v_exp)
                    if i >= 2:
                        t.wait_ge(sem_v.h, vmark_bmep[kt - 2])
                    sem_t.inc(t.transpose(PALL[:, par * 512:par * 512 + 128],
                                          BME2F[:, i * 128:(i + 1) * 128], IDF[:]))
                    tmark_tp[(p2, i)] = sem_t.n
                block.tensor(tkt)
                tm = sem_t.n

                def ekt(v, kt=kt, par=par, tm=tm):
                    v.wait_ge(sem_t.h, tm)
                    sem_v.inc(v.tensor_copy(BMEP[:, kt * 128:(kt + 1) * 128],
                                            PALL[:, par * 512:par * 512 + 128]))
                block.vector(ekt)
                vmark_bmep[kt] = sem_v.n
        t_prep_done = sem_t.n

        pall_cm.__exit__(None, None, None)
        PB = [st.enter_context(nc.psum_tensor(f"PB{i}", [128, 512], F32))
              for i in range(8)]

        def emit_load(hf, t_xin_guard, stat_hook=None):
            v_entry, s_entry = sem_v.n, sem_s.n
            for qi in range(16):
                tt, q = qi // 4, qi % 4
                par = qi % 2

                c_ent_l = (sem_c[0].n, sem_c[1].n)

                def ldma(sy, hf=hf, tt=tt, q=q, qi=qi, par=par,
                         t_xin_guard=t_xin_guard, c_ent_l=c_ent_l):
                    if qi < 2:
                        sy.wait_ge(sem_t.h, t_xin_guard)
                        if c_ent_l[0]:
                            sy.wait_ge(sem_c[0].h, c_ent_l[0])
                        if c_ent_l[1]:
                            sy.wait_ge(sem_c[1].h, c_ent_l[1])
                    else:
                        sy.wait_ge(sem_t.h, mk[("t_xin", hf, qi - 2)])
                    row0 = hf * 512 + tt * 128
                    sem_d[par].inc(sy.dma_start(
                        XINB[par][:],
                        x_in[row0:row0 + 128, q * 1024:(q + 1) * 1024]), 16)
                block.sync(ldma)
                d_need = sem_d[par].n

                def ltp(t, hf=hf, qi=qi, par=par, d_need=d_need,
                        v_entry=v_entry, s_entry=s_entry):
                    t.wait_ge(sem_d[par].h, d_need)
                    if qi == 0:
                        t.wait_ge(sem_v.h, v_entry)
                        t.wait_ge(sem_s.h, s_entry)
                    else:
                        t.wait_ge(sem_v.h, mk[("v_xev", hf, qi - 1)])
                        t.wait_ge(sem_s.h, mk[("s_xev", hf, qi - 1)])
                    for j in range(8):
                        bank = PB[6] if j < 4 else PB[7]
                        ins = t.transpose(bank[:, (j % 4) * 128:(j % 4) * 128 + 128],
                                          XINB[par][:, j * 128:(j + 1) * 128], IDF[:])
                        if j == 3 or j == 7:
                            sem_t.inc(ins)
                    mk[("t_xin", hf, qi)] = sem_t.n
                block.tensor(ltp)
                t_need = sem_t.n

                def lev(v, hf=hf, tt=tt, q=q, qi=qi, t_need=t_need):
                    v.wait_ge(sem_t.h, t_need - 1)
                    ins = None
                    for j in range(4):
                        dk = q * 8 + j
                        ins = v.tensor_copy(
                            R[:, dk * TH + tt * 128: dk * TH + tt * 128 + 128],
                            PB[6][:, j * 128:(j + 1) * 128])
                    sem_v.inc(ins)
                    mk[("v_xev", hf, qi)] = sem_v.n
                block.vector(lev)

                def les(s, hf=hf, tt=tt, q=q, qi=qi, t_need=t_need):
                    s.wait_ge(sem_t.h, t_need)
                    ins = None
                    for j in range(4):
                        dk = q * 8 + 4 + j
                        ins = s.copy(
                            R[:, dk * TH + tt * 128: dk * TH + tt * 128 + 128],
                            PB[7][:, j * 128:(j + 1) * 128])
                    sem_s.inc(ins)
                    mk[("s_xev", hf, qi)] = sem_s.n
                block.scalar(les)
                if stat_hook is not None and qi >= 12:
                    q8 = (qi - 12) * 8
                    for dk in range(q8, q8 + 8):
                        stat_hook(dk, v_rgate=mk[("v_xev", hf, qi)],
                                  s_rgate=mk[("s_xev", hf, qi)])

        def ln_make(hf, sub):
            sa, bb_ = (4, 6) if sub == 1 else (6, 4)
            PSa, PSb = PB[sa], PB[sa + 1]
            PBa, PBb = PB[bb_], PB[bb_ + 1]
            s_entry = sem_s.n
            v_entry = sem_v.n
            t_entry = sem_t.n
            tmark_cast = []
            state = {"done": 0}

            def hook(dk, v_rgate=None, s_rgate=None):
                par = dk % 2

                def cst(s, dk=dk, par=par):
                    if v_rgate is not None:
                        s.wait_ge(sem_v.h, v_rgate)
                    if s_rgate is not None:
                        s.wait_ge(sem_s.h, s_rgate)
                    if dk == 0:
                        s.wait_ge(sem_s.h, s_entry)
                        if v_rgate is None:
                            s.wait_ge(sem_v.h, v_entry)
                    if dk < 2:
                        s.wait_ge(sem_t.h, t_entry)
                    else:
                        s.wait_ge(sem_t.h, tmark_cast[dk - 2])
                    sem_s.inc(s.activation(XB[:, par * TH:(par + 1) * TH],
                                           R[:, dk * TH:(dk + 1) * TH], AF.Copy))
                    sem_s.inc(s.activation(XQ[:, par * TH:(par + 1) * TH],
                                           R[:, dk * TH:(dk + 1) * TH], AF.Square))
                block.scalar(cst)
                s_need = sem_s.n

                def stm(t, dk=dk, par=par, s_need=s_need):
                    t.wait_ge(sem_s.h, s_need)
                    if dk == 0:
                        t.wait_ge(sem_v.h, v_entry)
                        t.wait_ge(sem_s.h, s_entry)
                    t.matmul(PSa[0:1, :], ONE1[:], XB[:, par * TH:(par + 1) * TH],
                             start=(dk == 0), stop=(dk == NDK - 1))
                    ins = t.matmul(PSb[0:1, :], ONE1[:],
                                   XQ[:, par * TH:(par + 1) * TH],
                                   start=(dk == 0), stop=(dk == NDK - 1))
                    sem_t.inc(ins)
                block.tensor(stm)
                tmark_cast.append(sem_t.n)
                state["done"] += 1

            def finalize(t_xn_guard, v_banks_guard=None, s_banks_guard=None):
                assert state["done"] == NDK
                t_stats = sem_t.n
                a_guard = (mk.get("a_out0", 0), mk.get("a_out1", 0))
                ST0 = ST[:, 0:512]
                ST1 = ST[:, 512:1024]
                ST2 = ST[:, 1024:1536]

                def stats_v1(v):
                    v.wait_ge(sem_t.h, t_stats)
                    v.wait_ge(sem_v.h, sem_v.n)
                    if a_guard[0]:
                        v.wait_ge(sem_a[0].h, a_guard[0])
                    if a_guard[1]:
                        v.wait_ge(sem_a[1].h, a_guard[1])
                    sem_v.inc(v.tensor_scalar_mul(ST0, PSa[0:1, :], 1.0 / D))
                    sem_v.inc(v.tensor_scalar_mul(ST1, PSb[0:1, :], 1.0 / D))
                    v.wait_ge(sem_v.h, sem_v.n)
                    sem_v.inc(v.tensor_mul(ST2, ST0, ST0))
                    v.wait_ge(sem_v.h, sem_v.n)
                    sem_v.inc(v.tensor_sub(ST1, ST1, ST2))
                    v.wait_ge(sem_v.h, sem_v.n)
                    sem_v.inc(v.tensor_scalar_add(ST1, ST1, NEPS))
                    v.wait_ge(sem_v.h, sem_v.n)
                    sem_v.inc(v.reciprocal(ST2, ST1))
                block.vector(stats_v1)
                v_need = sem_v.n

                def stats_s(s):
                    s.wait_ge(sem_v.h, v_need)
                    sem_s.inc(s.activation(ST2, ST2, AF.Sqrt))
                block.scalar(stats_s)
                s_need = sem_s.n

                def stats_v2(v):
                    v.wait_ge(sem_s.h, s_need)
                    if sub == 2:
                        sem_v.inc(v.tensor_mul(ST0, ST0, ST2))
                        v.wait_ge(sem_v.h, sem_v.n)
                    sem_v.inc(v.tensor_scalar_mul(ST0, ST0, -1.0))
                block.vector(stats_v2)
                v_stat2 = sem_v.n

                def bc_t(t):
                    t.wait_ge(sem_v.h, v_stat2)
                    if v_banks_guard:
                        t.wait_ge(sem_v.h, v_banks_guard)
                    if s_banks_guard:
                        t.wait_ge(sem_s.h, s_banks_guard)
                    t.matmul(PBa[:], ONEK[0:1, :], ST2, start=True, stop=True)
                    ins = t.matmul(PBb[:], ONEK[0:1, :], ST0, start=True, stop=True)
                    sem_t.inc(ins)
                block.tensor(bc_t)
                t_bc = sem_t.n

                def bc_v(v):
                    v.wait_ge(sem_t.h, t_bc)
                    sem_v.inc(v.tensor_copy(RB[:], PBa[:]))
                    sem_v.inc(v.tensor_copy(MB[:], PBb[:]))
                block.vector(bc_v)
                v_bc = sem_v.n
                mk[("v_bc", hf, sub)] = v_bc

                def apply_v(v):
                    v.wait_ge(sem_v.h, v_bc)
                    if t_xn_guard:
                        v.wait_ge(sem_t.h, t_xn_guard)
                    for dk in range(NDK):
                        if sub == 2:
                            sem_v.inc(v.tensor_add(GS[:], R[:, dk * TH:(dk + 1) * TH],
                                                   MB[:]))
                            v.wait_ge(sem_v.h, sem_v.n)
                            sem_v.inc(v.tensor_mul(XN[:, dk * TH:(dk + 1) * TH],
                                                   GS[:], RB[:]))
                            v.wait_ge(sem_v.h, sem_v.n)
                        else:
                            sem_v.inc(v.tensor_add(XN[:, dk * TH:(dk + 1) * TH],
                                                   R[:, dk * TH:(dk + 1) * TH],
                                                   MB[:]))
                        mk[("v_xn_dk", hf, sub, dk)] = sem_v.n
                block.vector(apply_v)
                mk[("v_xn", hf, sub)] = sem_v.n

            return hook, finalize

        def emit_ffn(hf, pas, t_gu_region, v_gu_region, t_d_region,
                     t_up_region=None, stat_hook=None, store_hook=None):
            v_xn = mk[("v_xn", hf, pas)]
            a_ent = (sem_a[0].n, sem_a[1].n)
            c_ent = (sem_c[0].n, sem_c[1].n)
            if t_up_region is None:
                t_up_region = t_gu_region
            for third, (ik0, ik1) in enumerate(THIRDS):
                nik = ik1 - ik0
                for ib in range(ik0, ik1):
                    par = ib % 2

                    def wdma(gp, hf=hf, pas=pas, ib=ib, par=par, third=third,
                             ik0=ik0):
                        if ib < 2:
                            gp.wait_ge(sem_t.h, t_gu_region)
                            if v_gu_region:
                                gp.wait_ge(sem_v.h, v_gu_region)
                            if a_ent[0]:
                                gp.wait_ge(sem_a[0].h, a_ent[0])
                            if a_ent[1]:
                                gp.wait_ge(sem_a[1].h, a_ent[1])
                            if c_ent[0]:
                                gp.wait_ge(sem_c[0].h, c_ent[0])
                            if c_ent[1]:
                                gp.wait_ge(sem_c[1].h, c_ent[1])
                        elif ib < ik0 + 2:
                            gp.wait_ge(sem_t.h, mk[("t_d", hf, pas, third * 32 - 12)])
                        else:
                            gp.wait_ge(sem_t.h, mk[("t_gu", hf, pas, ib - 2)])
                        sem_g[par].inc(gp.dma_start(GB[par][:], wg[ib]), 16)
                        if ib < 2:
                            gp.wait_ge(sem_t.h, t_up_region)
                        sem_g[par].inc(gp.dma_start(UB[par][:], wu[ib]), 16)
                    block.gpsimd(wdma)
                    gneed = sem_g[par].n

                    def gu_t(t, hf=hf, pas=pas, ib=ib, par=par, gneed=gneed,
                             v_xn=v_xn, ik0=ik0, third=third):
                        t.wait_ge(sem_g[par].h, gneed)
                        if ib == 1:
                            t.wait_ge(sem_v.h, v_xn)
                        elif ib >= 2:
                            t.wait_ge(sem_v.h, mk[("v_guev", hf, pas, ib - 2)])
                        ins = None
                        for dk in range(NDK):
                            if ib == 0:
                                t.wait_ge(sem_v.h, mk[("v_xn_dk", hf, pas, dk)])
                            ins = t.matmul(PB[0 + 2 * par][:],
                                           GB[par][:, dk * 128:(dk + 1) * 128],
                                           XN[:, dk * TH:(dk + 1) * TH],
                                           start=(dk == 0), stop=(dk == NDK - 1))
                        sem_t.inc(ins)
                        for dk in range(NDK):
                            ins = t.matmul(PB[1 + 2 * par][:],
                                           UB[par][:, dk * 128:(dk + 1) * 128],
                                           XN[:, dk * TH:(dk + 1) * TH],
                                           start=(dk == 0), stop=(dk == NDK - 1))
                        sem_t.inc(ins)
                        mk[("t_gu", hf, pas, ib)] = sem_t.n
                    block.tensor(gu_t)
                    t_gate = sem_t.n - 1
                    t_up = sem_t.n

                    def gu_v1(v, hf=hf, pas=pas, ib=ib, par=par, t_gate=t_gate,
                              third=third, ik0=ik0):
                        v.wait_ge(sem_t.h, t_gate)
                        if third > 0 and ib == ik0:
                            v.wait_ge(sem_t.h, mk[("t_d", hf, pas, third * 32 - 1)])
                        v.wait_ge(sem_v.h, mk.get(("v_h", hf, pas, ib - 1), 0))
                        sem_v.inc(v.scalar_tensor_tensor(
                            GS[:], PB[0 + 2 * par][:], SGV[:, ib:ib + 1], RB[:],
                            ALU.mult, ALU.mult))
                    block.vector(gu_v1)
                    v_gs = sem_v.n

                    def gu_s(s, v_gs=v_gs):
                        s.wait_ge(sem_v.h, v_gs)
                        sem_s.inc(s.activation(SG[:], GS[:], AF.Silu))
                    block.scalar(gu_s)
                    s_sg = sem_s.n

                    def gu_v2(v, hf=hf, pas=pas, ib=ib, par=par, t_up=t_up,
                              s_sg=s_sg, ik0=ik0):
                        v.wait_ge(sem_t.h, t_up)
                        sem_v.inc(v.scalar_tensor_tensor(
                            US[:], PB[1 + 2 * par][:], SUV[:, ib:ib + 1], RB[:],
                            ALU.mult, ALU.mult))
                        mk[("v_guev", hf, pas, ib)] = sem_v.n
                        v.wait_ge(sem_v.h, sem_v.n)
                        v.wait_ge(sem_s.h, s_sg)
                        ikl = ib - ik0
                        sem_v.inc(v.tensor_mul(H[:, ikl * TH:(ikl + 1) * TH],
                                               SG[:], US[:]))
                        mk[("v_h", hf, pas, ib)] = sem_v.n
                    block.vector(gu_v2)
                if third == 2:
                    mk[("t_after_gu", hf, pas)] = sem_t.n
                v_h_third = sem_v.n

                for ot in range(32):
                    j = third * 32 + ot
                    spar = j % 3 if j < 64 else j % 2
                    bpar = j % 2

                    def ddma(gp, hf=hf, pas=pas, j=j, ot=ot, spar=spar, ik0=ik0,
                             ik1=ik1):
                        if j < 3:
                            gp.wait_ge(sem_t.h, t_d_region)
                            gp.wait_ge(sem_v.h, mk[("v_bc", hf, pas)])
                            if a_ent[0]:
                                gp.wait_ge(sem_a[0].h, a_ent[0])
                            if a_ent[1]:
                                gp.wait_ge(sem_a[1].h, a_ent[1])
                        elif j in (64, 65):
                            gp.wait_ge(sem_t.h, mk[("t_d", hf, pas, 63)])
                        elif j >= 66:
                            gp.wait_ge(sem_t.h, mk[("t_d", hf, pas, j - 2)])
                        else:
                            gp.wait_ge(sem_t.h, mk[("t_d", hf, pas, j - 3)])
                        sem_n[spar].inc(gp.dma_start(
                            DB[spar][:, 0:(ik1 - ik0) * 128],
                            wd[ot][:, ik0:ik1, :]), 16)
                    block.gpsimd(ddma)
                    gneed = sem_n[spar].n

                    def d_t(t, hf=hf, pas=pas, third=third, ot=ot, j=j, spar=spar,
                            bpar=bpar, gneed=gneed, nik=nik, v_h_third=v_h_third):
                        t.wait_ge(sem_n[spar].h, gneed)
                        if ot == 0:
                            t.wait_ge(sem_v.h, v_h_third)
                        if j < 2:
                            t.wait_ge(sem_v.h, mk[("v_bc", hf, pas)])
                        else:
                            t.wait_ge(sem_v.h, mk[("v_dev", hf, pas, j - 2)])
                        ins = None
                        for ik in range(nik):
                            ins = t.matmul(PB[4 + bpar][:],
                                           DB[spar][:, ik * 128:(ik + 1) * 128],
                                           H[:, ik * TH:(ik + 1) * TH],
                                           start=(ik == 0), stop=(ik == nik - 1))
                        sem_t.inc(ins)
                        mk[("t_d", hf, pas, j)] = sem_t.n
                    block.tensor(d_t)
                    t_dj = sem_t.n

                    def d_v(v, hf=hf, pas=pas, ot=ot, j=j, bpar=bpar, t_dj=t_dj):
                        v.wait_ge(sem_t.h, t_dj)
                        v.wait_ge(sem_v.h, sem_v.n)
                        sem_v.inc(v.scalar_tensor_tensor(
                            R[:, ot * TH:(ot + 1) * TH], PB[4 + bpar][:],
                            SDV[:, ot:ot + 1], R[:, ot * TH:(ot + 1) * TH],
                            ALU.mult, ALU.add))
                        mk[("v_dev", hf, pas, j)] = sem_v.n
                    block.vector(d_v)
                    if stat_hook is not None and third == 2 and ot >= 2:
                        stat_hook(ot - 2,
                                  v_rgate=mk[("v_dev", hf, pas, j - 2)])
                    if store_hook is not None and third == 2 and (ot + 1) % 8 == 0:
                        store_hook((ot - 7) // 8,
                                   v_rgate=mk[("v_dev", hf, pas, j)])
            if stat_hook is not None:
                stat_hook(30, v_rgate=mk[("v_dev", hf, pas, 94)])
                stat_hook(31, v_rgate=mk[("v_dev", hf, pas, 95)])

        def emit_attn(hf, stat_hook=None):
            v_xn = mk[("v_xn", hf, 2)]
            t_gu1 = mk[("t_after_gu", hf, 1)]
            for nt in range(32):
                for kh in range(2):
                    sl = nt * 2 + kh
                    par = sl % 2
                    wp = sl % 4
                    qsem = (sem_d[0] if sl % 2 == 0 else sem_a[1]) if hf == 0 else \
                        (sem_d[0], sem_a[1], sem_d[1], sem_a[0])[sl % 4]
                    qblk = (block.sync if sl % 2 == 0 else block.scalar) if hf == 0 \
                        else (block.sync, block.scalar, block.sync,
                              block.scalar)[sl % 4]

                    if hf == 0:
                        def adma(eng, hf=hf, nt=nt, kh=kh, sl=sl, par=par,
                                 qsem=qsem, t_gu1=t_gu1):
                            if sl < 2:
                                eng.wait_ge(sem_t.h, t_gu1)
                            else:
                                eng.wait_ge(sem_v.h, mk[("v_msk", hf, sl - 2)])
                            qsem.inc(eng.dma_start(
                                AB[par][:], wa[nt][:, kh * 16:(kh + 1) * 16, :]), 16)
                        qblk(adma)
                        dneed = qsem.n

                        def msk(v, hf=hf, nt=nt, kh=kh, sl=sl, par=par,
                                dneed=dneed, qsem=qsem, t_gu1=t_gu1):
                            v.wait_ge(qsem.h, dneed)
                            if sl < 4:
                                v.wait_ge(sem_t.h, t_gu1)
                            else:
                                v.wait_ge(sem_t.h, mk[("t_am", hf, sl - 4)])
                                v.wait_ge(sem_c[sl % 2].h, mk[("a_wtm", hf, sl - 4)])
                            ins = None
                            for q in range(4):
                                kk0 = kh * 16 + q * 4
                                ab = AB[par][:]
                                wt = WTM[wp][:]
                                bp = BMEP[:]
                                in0 = bass.AP(ab.tensor, ab.offset + q * 512,
                                              [ab.ap[0], [128, 4], [32, 4], [1, 32]])
                                o0 = bass.AP(wt.tensor, wt.offset + q * 512,
                                             [wt.ap[0], [128, 4], [32, 4], [1, 32]])
                                i1 = bass.AP(bp.tensor,
                                             bp.offset + kk0 * 128 + nt * 4,
                                             [bp.ap[0], [128, 4], [1, 4], [0, 32]])
                                ins = v.tensor_tensor(o0, in0, i1, ALU.mult)
                            sem_v.inc(ins)
                            mk[("v_msk", hf, sl)] = sem_v.n
                        block.vector(msk)
                        v_m = sem_v.n

                        def cwr(s, hf=hf, nt=nt, kh=kh, sl=sl, wp=wp, v_m=v_m):
                            s.wait_ge(sem_v.h, v_m)
                            if sl >= 2:
                                s.wait_ge(sem_c[sl % 2].h, mk[("a_wtm", hf, sl - 2)])
                            sem_c[sl % 2].inc(
                                s.dma_start(wtmc[nt][kh], WTM[wp][:]), 16)
                            mk[("a_wtm", hf, sl)] = sem_c[sl % 2].n
                        block.scalar(cwr)
                        t_wtm_need = ("v", v_m)
                    else:
                        def adma(eng, hf=hf, nt=nt, kh=kh, sl=sl, wp=wp,
                                 qsem=qsem, t_gu1=t_gu1):
                            if sl < 4:
                                eng.wait_ge(sem_t.h, t_gu1)
                                eng.wait_ge(sem_c[sl % 2].h, mk[("a_wtm", 0, 62 + sl % 2)])
                            else:
                                eng.wait_ge(sem_t.h, mk[("t_am", hf, sl - 4)])
                            qsem.inc(eng.dma_start(
                                WTM[wp][:], wtmc[nt][kh]), 16)
                        qblk(adma)
                        t_wtm_need = ("q", qsem.n, qsem)

                    def amm(t, hf=hf, nt=nt, kh=kh, sl=sl, wp=wp,
                            t_wtm_need=t_wtm_need):
                        if t_wtm_need[0] == "v":
                            t.wait_ge(sem_v.h, t_wtm_need[1])
                        else:
                            t.wait_ge(t_wtm_need[2].h, t_wtm_need[1])
                        if nt == 1 and kh == 0:
                            t.wait_ge(sem_v.h, mk[("v_xn", hf, 2)])
                        elif nt >= 2 and kh == 0:
                            t.wait_ge(sem_v.h, mk[("v_aev", hf, nt - 2)])
                        ins = None
                        for k16 in range(16):
                            kk = kh * 16 + k16
                            if nt == 0:
                                t.wait_ge(sem_v.h, mk[("v_xn_dk", hf, 2, kk)])
                            ins = t.matmul(PB[4 + nt % 2][:],
                                           WTM[wp][:, k16 * 128:(k16 + 1) * 128],
                                           XN[:, kk * TH:(kk + 1) * TH],
                                           start=(kh == 0 and k16 == 0),
                                           stop=(kh == 1 and k16 == 15))
                        sem_t.inc(ins)
                        mk[("t_am", hf, sl)] = sem_t.n
                    block.tensor(amm)
                t_nt = sem_t.n

                def aev(v, hf=hf, nt=nt, t_nt=t_nt):
                    v.wait_ge(sem_t.h, t_nt)
                    v.wait_ge(sem_v.h, sem_v.n)
                    sem_v.inc(v.scalar_tensor_tensor(
                        R[:, nt * TH:(nt + 1) * TH], PB[4 + nt % 2][:],
                        BIA[:, nt:nt + 1], R[:, nt * TH:(nt + 1) * TH],
                        ALU.add, ALU.add))
                    mk[("v_aev", hf, nt)] = sem_v.n
                block.vector(aev)
                if stat_hook is not None and nt >= 2:
                    stat_hook(nt - 2, v_rgate=mk[("v_aev", hf, nt - 2)])
            if stat_hook is not None:
                stat_hook(30, v_rgate=mk[("v_aev", hf, 30)])
                stat_hook(31, v_rgate=mk[("v_aev", hf, 31)])
            mk[("t_after_attn", hf)] = sem_t.n

        def make_store(hf):
            s_entry = sem_s.n
            a_entry = (mk.get("a_out0", 0), mk.get("a_out1", 0))
            state = {"ci": 0}

            def hook(dq, v_rgate):
                for tt in range(4):
                    ci = state["ci"]
                    par = ci % 2

                    def stp(t, hf=hf, tt=tt, dq=dq, ci=ci, v_rgate=v_rgate,
                            s_entry=s_entry):
                        t.wait_ge(sem_v.h, v_rgate)
                        if ci == 0:
                            t.wait_ge(sem_s.h, s_entry)
                        else:
                            t.wait_ge(sem_v.h, mk[("v_oev", hf, ci - 1)])
                            t.wait_ge(sem_s.h, mk[("s_oev", hf, ci - 1)])
                        for j in range(8):
                            dk = dq * 8 + j
                            bank = PB[6] if j < 4 else PB[7]
                            ins = t.transpose(
                                bank[:, (j % 4) * 128:(j % 4) * 128 + 128],
                                R[:, dk * TH + tt * 128: dk * TH + tt * 128 + 128],
                                IDF[:])
                            if j == 3 or j == 7:
                                sem_t.inc(ins)
                    block.tensor(stp)
                    t_need = sem_t.n

                    def sev(v, hf=hf, ci=ci, par=par, t_need=t_need,
                            a_entry=a_entry):
                        v.wait_ge(sem_t.h, t_need - 1)
                        v.wait_ge(sem_a[par].h, mk[("a_ost", hf, ci - 2)]
                                  if ci >= 2 else a_entry[par])
                        sem_v.inc(v.tensor_copy(OUTB[par][:, 0:512],
                                                PB[6][:, 0:512]))
                        mk[("v_oev", hf, ci)] = sem_v.n
                    block.vector(sev)
                    v_need = sem_v.n

                    def ses(s, hf=hf, tt=tt, dq=dq, ci=ci, par=par,
                            t_need=t_need, v_need=v_need, a_entry=a_entry):
                        s.wait_ge(sem_t.h, t_need)
                        s.wait_ge(sem_a[par].h, mk[("a_ost", hf, ci - 2)]
                                  if ci >= 2 else a_entry[par])
                        sem_s.inc(s.copy(OUTB[par][:, 512:1024], PB[7][:, 0:512]))
                        mk[("s_oev", hf, ci)] = sem_s.n
                        s.wait_ge(sem_s.h, mk[("s_oev", hf, ci)])
                        s.wait_ge(sem_v.h, v_need)
                        row0 = hf * 512 + tt * 128
                        sem_a[par].inc(s.dma_start(
                            out_d[row0:row0 + 128, dq * 1024:(dq + 1) * 1024],
                            OUTB[par][:]), 16)
                        mk[("a_ost", hf, ci)] = sem_a[par].n
                    block.scalar(ses)
                    state["ci"] += 1
            return hook

        for hf in range(2):
            t_xin_guard = t_prep_done if hf == 0 else mk[("t_after_gu", 0, 3)]
            ln1_hook, ln1_fin = ln_make(hf, 1)
            emit_load(hf, t_xin_guard, stat_hook=ln1_hook)
            t_load = sem_t.n
            ln1_fin(t_xn_guard=(0 if hf == 0 else mk[("t_after_gu", 0, 3)]),
                    v_banks_guard=mk[("v_xev", hf, 15)],
                    s_banks_guard=mk[("s_xev", hf, 15)])
            ln2_hook, ln2_fin = ln_make(hf, 2)
            emit_ffn(hf, 1,
                     t_gu_region=t_load,
                     v_gu_region=(0 if hf == 0 else mk[("v_msk", 0, 63)]),
                     t_d_region=t_load,
                     stat_hook=ln2_hook)
            ln2_fin(t_xn_guard=mk[("t_after_gu", hf, 1)],
                    v_banks_guard=mk[("v_dev", hf, 1, 95)])
            ln3_hook, ln3_fin = ln_make(hf, 3)
            emit_attn(hf, stat_hook=ln3_hook)
            ln3_fin(t_xn_guard=mk[("t_after_attn", hf)],
                    v_banks_guard=mk[("v_aev", hf, 31)])
            st_hook = make_store(hf)
            emit_ffn(hf, 3,
                     t_gu_region=(mk[("t_after_attn", hf)] if hf == 0
                                  else mk[("t_after_gu", hf, 1)]),
                     v_gu_region=(mk[("v_msk", hf, 63)] if hf == 0 else 0),
                     t_d_region=mk[("t_d", hf, 1, 95)],
                     t_up_region=mk[("t_after_attn", hf)],
                     store_hook=st_hook)
            mk["a_out0"] = sem_a[0].n
            mk["a_out1"] = sem_a[1].n

        a_all = (sem_a[0].n, sem_a[1].n)

        def fin(s, a_all=a_all):
            s.wait_ge(sem_a[0].h, a_all[0])
            s.wait_ge(sem_a[1].h, a_all[1])
        block.scalar(fin)

    return nc


def _layout_inputs(L0_attn_data, L0_attn_mask, L0_attn_weight, L0_attn_bias,
                   L0_ffn_w_gate, L0_ffn_s_gate, L0_ffn_w_up, L0_ffn_s_up,
                   L0_ffn_w_down, L0_ffn_s_down):
    x = np.ascontiguousarray(np.asarray(L0_attn_data, dtype=np.float32)
                             .reshape(8192, D))
    wg = np.ascontiguousarray(np.asarray(L0_ffn_w_gate, dtype=np.int8)
                              .reshape(NIB, 128, NDK, 128).transpose(0, 3, 2, 1))
    wu = np.ascontiguousarray(np.asarray(L0_ffn_w_up, dtype=np.int8)
                              .reshape(NIB, 128, NDK, 128).transpose(0, 3, 2, 1))
    wd = np.ascontiguousarray(np.asarray(L0_ffn_w_down, dtype=np.int8)
                              .reshape(32, 128, NIB, 128).transpose(0, 3, 2, 1))
    wa = np.ascontiguousarray(np.asarray(L0_attn_weight, dtype=np.float32)
                              .reshape(32, 128, 32, 128).transpose(0, 3, 2, 1))
    mask = np.ascontiguousarray(np.asarray(L0_attn_mask, dtype=np.int32))
    sgv = np.ascontiguousarray(np.asarray(L0_ffn_s_gate, dtype=np.float32)
                               .reshape(NIB, 128).T)
    suv = np.ascontiguousarray(np.asarray(L0_ffn_s_up, dtype=np.float32)
                               .reshape(NIB, 128).T)
    sdv = np.ascontiguousarray(np.asarray(L0_ffn_s_down, dtype=np.float32)
                               .reshape(32, 128).T)
    bia = np.ascontiguousarray(np.asarray(L0_attn_bias, dtype=np.float32)
                               .reshape(32, 128).T)
    idf = np.eye(128, dtype=np.float32)
    idb = np.eye(128, dtype=np.float32).astype(BF16NP)
    one1 = np.ones((128, 1), dtype=np.float32).astype(BF16NP)
    onek = np.ones((1, 128), dtype=np.float32)
    indr = np.zeros((128, 32 * 128), dtype=np.float32)
    p = np.arange(128)
    for rt in range(32):
        indr[p, rt * 128 + rt * 4 + p // 32] = 1.0
    indr = indr.astype(BF16NP)
    shared = dict(wg=wg, wu=wu, wd=wd, wa=wa, mask=mask, sgv=sgv, suv=suv,
                  sdv=sdv, bia=bia, idf=idf, idb=idb, one1=one1, onek=onek,
                  indr=indr)
    in_maps = []
    for c in range(NCORE):
        m = dict(shared)
        m["x"] = np.ascontiguousarray(x[c * 1024:(c + 1) * 1024])
        in_maps.append(m)
    return in_maps


def kernel(L0_attn_data, L0_attn_mask, L0_attn_weight, L0_attn_bias,
           L0_ffn_w_gate, L0_ffn_s_gate, L0_ffn_w_up, L0_ffn_s_up,
           L0_ffn_w_down, L0_ffn_s_down):
    in_maps = _layout_inputs(L0_attn_data, L0_attn_mask, L0_attn_weight,
                             L0_attn_bias, L0_ffn_w_gate, L0_ffn_s_gate,
                             L0_ffn_w_up, L0_ffn_s_up, L0_ffn_w_down,
                             L0_ffn_s_down)
    nc = build_nc()
    res = run_bass_kernel_spmd(nc, in_maps, core_ids=list(range(NCORE)))
    out = np.concatenate([np.asarray(res.results[c]["out"]) for c in range(NCORE)],
                         axis=0)
    return np.ascontiguousarray(out.reshape(2, 4096, D), dtype=np.float32)
